# version 30
# baseline (speedup 1.0000x reference)
import contextlib
import numpy as np

import concourse.bass as bass
import concourse.mybir as mybir
from concourse.bass_utils import run_bass_kernel_spmd
from concourse.dve_op import DveOp
from concourse import dve_ops as _dve_ops
from concourse.dve_spec import Spec, Src0, Src1, C0, C1, C2, lower as _dve_lower

F32 = mybir.dt.float32
BF16 = mybir.dt.bfloat16
I32 = mybir.dt.int32
AF = mybir.ActivationFunctionType
ALU = mybir.AluOpType

NCORES = 8

_NF4 = np.array([
    -1.0, -0.6961928009986877, -0.5250730514526367, -0.39491748809814453,
    -0.28444138169288635, -0.18477343022823334, -0.09105003625154495, 0.0,
    0.07958029955625534, 0.16093020141124725, 0.2461123019456863,
    0.33791524171829224, 0.44070982933044434, 0.5626170039176941,
    0.7229568362236023, 1.0], dtype=np.float64)

VSCALE = 0.25
VSHIFT = -7.5 * 0.25


def _nf4_chain_coeffs():
    c = np.arange(16, dtype=np.float64)
    v = c * VSCALE + VSHIFT
    V = np.vander(v, 16, increasing=True)
    p = np.linalg.solve(V, _NF4)
    for _ in range(3):
        p += np.linalg.solve(V, _NF4 - V @ p)
    a0 = float(p[0])
    bs = [float(x) for x in p[1:][::-1]]
    return bs, a0


_DVE_REG = {}


def _reg_dve(name, body, perf_en, ref):
    if name in _DVE_REG:
        return _DVE_REG[name]
    spec = Spec(body=body)
    shas = {}
    pe = {}
    for ver in ("v3", "v4"):
        try:
            shas[ver] = _dve_lower(spec, ver=ver, perf_en=perf_en, name=name).sha(ver)
            pe[ver] = perf_en
        except Exception:
            try:
                shas[ver] = _dve_lower(spec, ver=ver, perf_en=False, name=name).sha(ver)
                pe[ver] = False
            except Exception:
                pass
    op = DveOp(name, spec, subdim=False, uops_sha=shas, perf_en=pe, reference=ref)
    _dve_ops.OPS.append(op)
    _dve_ops._SUB_OPCODE_FOR_NAME[name] = _dve_ops._CUSTOM_DVE_ROW_BASE + len(_DVE_REG)
    _dve_ops.CUSTOM_DVE_REFS[name] = ref
    if hasattr(_dve_ops, "CUSTOM_DVE_SPECS"):
        _dve_ops.CUSTOM_DVE_SPECS[name] = spec
    _DVE_REG[name] = op
    return op


def _nf4_dve_ops():
    d_init = _reg_dve(
        "NF4_H_INIT", (Src0 * C0 + C1) * Src0 + C2, True,
        lambda in0, in1, s0, s1, imm2: ((np.float32(s0) * in0 + np.float32(s1))
                                        * in0 + np.float32(imm2)).astype(np.float32))
    d_step = _reg_dve(
        "NF4_H_STEP", ((Src1 * Src0 + C0) * Src0 + C1) * Src0 + C2, False,
        lambda in0, in1, s0, s1, imm2: (((in1 * in0 + np.float32(s0)) * in0
                                         + np.float32(s1)) * in0
                                        + np.float32(imm2)).astype(np.float32))
    d_tail = _reg_dve(
        "NF4_H_TAIL", (((Src1 * Src0 + C0) * Src0 + C1) * Src0 + C2) * Src0, False,
        lambda in0, in1, s0, s1, imm2: ((((in1 * in0 + np.float32(s0)) * in0
                                          + np.float32(s1)) * in0
                                         + np.float32(imm2)) * in0).astype(np.float32))
    return d_init, d_step, d_tail


def _chop(total, maxn, align=1):
    out = []
    s = 0
    while s < total:
        n = min(maxn, total - s)
        if n < total - s:
            n -= n % align
        out.append((s, n))
        s += n
    return out


class Cfg:
    def __init__(self, DM, DFF, T, FGMAX=512, OGMAX=512, QKMAX=2048, TGT=4, ncores=NCORES):
        self.DM, self.DFF, self.T = DM, DFF, T
        self.R = 16
        self.KC = DM // 128
        self.FC = DFF // 128
        self.TC = T // 128
        self.TH = T // 2
        self.THC = self.TH // 128
        self.FGROUPS = _chop(DFF, FGMAX)
        self.OGROUPS = _chop(DM, OGMAX)
        self.FGMAX = max(n for _, n in self.FGROUPS)
        self.OGMAX = max(n for _, n in self.OGROUPS)
        self.QK_GU = _chop(DM, QKMAX, 64)
        self.QK_D = _chop(DFF, QKMAX, 64)
        self.QKMAX = min(QKMAX, max(max(n for _, n in self.QK_GU),
                                    max(n for _, n in self.QK_D)))
        self.TGT = TGT
        self.NT = DM // 128
        self.TPIECES = _chop(T, 512)
        self.NCORES = ncores
        self.GU_SH_CH = -(-self.FC // ncores)
        self.D_SH_CH = -(-self.KC // ncores)
        self.GU_PAD_CH = self.GU_SH_CH * ncores
        self.D_PAD_CH = self.D_SH_CH * ncores


def band_tiles_from_mask_shape(DM):
    import math
    stride = int(math.sqrt(DM))
    m = np.zeros((DM, DM), dtype=bool)
    for k in range(3):
        ps = stride * k
        for i in range(0, DM, 6):
            m[ps + i:ps + i + 6, i:i + 6] = True
    m |= m.T
    nt = DM // 128
    return [(s, t) for s in range(nt) for t in range(nt)
            if m[s * 128:(s + 1) * 128, t * 128:(t + 1) * 128].any()]


_BUILD_N = [0]


class B:
    ENGINES = ("pe", "ve", "sc", "gp", "sp")

    def __init__(self, nc):
        self.nc = nc
        self.ops = {e: [] for e in self.ENGINES}
        self.count = {}
        self.sem_handles = {}
        self.bar = []
        self.prev_tok = {"ve": None, "sc": None}
        self.es = contextlib.ExitStack()
        for s in self.ENGINES:
            self._mksem(s)

    def _mksem(self, s):
        if s not in self.sem_handles:
            self.sem_handles[s] = self.es.enter_context(
                self.nc.semaphore(f"s{s}_{_BUILD_N[0]}"))
            self.count[s] = 0

    def _op(self, eng, fn, deps, sem, inc):
        deps = [d for d in deps if d is not None] + self.bar
        if eng in self.prev_tok and self.prev_tok[eng] is not None:
            deps = deps + [self.prev_tok[eng]]
        self.count[sem] += inc
        tok = (sem, self.count[sem])
        self.ops[eng].append((deps, fn, sem, inc))
        if eng in self.prev_tok:
            self.prev_tok[eng] = tok
        return tok

    def eng(self, eng, fn, deps=()):
        return self._op(eng, fn, deps, eng, 1)

    def dma(self, q, out_ap, in_ap, deps=(), transpose=False, sem=None):
        sem = f"D{q}_{sem or 'x'}"
        self._mksem(sem)

        def fn(e, out_ap=out_ap, in_ap=in_ap, transpose=transpose):
            if transpose:
                return e.dma_start(out=out_ap, in_=in_ap, transpose=True)
            return e.dma_start(out=out_ap, in_=in_ap)
        return self._op(q, fn, deps, sem, 16)

    def collective(self, kind, op, groups, ins, outs, deps=(), sem=None):
        sem = f"C_{sem or 'c'}"
        self._mksem(sem)

        def fn(e, kind=kind, op=op, groups=groups, ins=ins, outs=outs):
            return e.collective_compute(kind, op, replica_groups=groups,
                                        ins=ins, outs=outs)
        return self._op("gp", fn, deps, sem, 1)

    def barrier(self):
        self.bar = [(s, v) for s, v in self.count.items() if v > 0]

    def finalize(self):
        with self.nc.Block() as block:
            for eng in self.ENGINES:
                if not self.ops[eng]:
                    continue
                deco = {"pe": block.tensor, "ve": block.vector, "sc": block.scalar,
                        "gp": block.gpsimd, "sp": block.sync}[eng]
                ops = self.ops[eng]
                sems = self.sem_handles

                def section(e, ops=ops, sems=sems):
                    waited = {}
                    for deps, fn, sem, inc in ops:
                        for (ds, dv) in deps:
                            if waited.get(ds, 0) < dv:
                                e.wait_ge(sems[ds], dv)
                                waited[ds] = dv
                        fn(e).then_inc(sems[sem], inc)
                deco(section)
        self.es.close()


def build_program(cfg, band_tiles):
    _BUILD_N[0] += 1
    tag = f"k{_BUILD_N[0]}"
    nc = bass.Bass("TRN2", target_bir_lowering=False)
    C = cfg
    DM, DFF, T, R = C.DM, C.DFF, C.T, C.R

    def din(name, shape, dtype):
        return nc.dram_tensor(name, shape, dtype, kind="ExternalInput").ap()

    def dint(name, shape, dtype):
        return nc.dram_tensor(f"{name}_{tag}", shape, dtype, kind="Internal").ap()

    x_in = din("x", [T, DM], F32)
    w1 = din("w1", [DM, DM], F32)
    w2 = din("w2", [DM, DM], F32)
    maskb = din("maskb", [len(band_tiles), 128, 128], F32)
    b2 = din("b2", [DM], F32)
    GUR = C.GU_SH_CH * 128
    DR = C.D_SH_CH * 128
    gc, ga = din("gc", [GUR, DM], I32), din("ga", [GUR, DM // 64], F32)
    uc, ua = din("uc", [GUR, DM], I32), din("ua", [GUR, DM // 64], F32)
    dc, da = din("dc", [DR, DFF], I32), din("da", [DR, DFF // 64], F32)
    gla, glb = din("gla", [DM, R], F32), din("glb", [R, DFF], F32)
    ula, ulb = din("ula", [DM, R], F32), din("ulb", [R, DFF], F32)
    dla, dlb = din("dla", [DFF, R], F32), din("dlb", [R, DM], F32)
    out = nc.dram_tensor("out", [T, DM], F32, kind="ExternalOutput").ap()

    x1tm = dint("x1tm", [T, DM], BF16)
    x3tm = dint("x3tm", [T, DFF], BF16)
    resid2 = dint("resid2", [T, DM], F32)
    resid3 = dint("resid3", [T, DM], F32)
    shard_g = dint("shard_g", [GUR, DM], BF16)
    shard_u = dint("shard_u", [GUR, DM], BF16)
    shard_d = dint("shard_d", [DR, DFF], BF16)
    gcache = dint("gcache", [C.GU_PAD_CH * 128, DM], BF16)
    ucache = dint("ucache", [C.GU_PAD_CH * 128, DM], BF16)
    dcache = dint("dcache", [C.D_PAD_CH * 128, DFF], BF16)
    w1b = dint("w1b", [DM, DM], BF16)
    w2b = dint("w2b", [DM, DM], BF16)

    b = B(nc)
    es = b.es

    def sb(name, shape, dtype):
        return es.enter_context(nc.sbuf_tensor(f"{name}_{tag}", shape, dtype))

    band_set = set(band_tiles)
    group_sigs = {}
    for g0 in range(0, C.NT, C.TGT):
        gts = list(range(g0, min(g0 + C.TGT, C.NT)))
        sigs = sorted({s for (s, t) in band_set if t in gts})
        group_sigs[g0] = (gts, sigs)
    nsig_max = max(len(s) for _, s in group_sigs.values())
    sspan_max = max(s[-1] - s[0] + 1 for _, s in group_sigs.values())
    assert nsig_max <= 8, f"attn needs {nsig_max} psum banks"

    AR = max(C.KC * T,
             C.FC * C.TH,
             2 * 2 * C.QKMAX + 2 * C.QKMAX + 2 * 2 * C.QKMAX + 2 * C.QKMAX + 256)
    TAIL = C.KC * T
    AR = max(AR, TAIL + max(
        C.TC * C.FGMAX + 2 * C.FGMAX + 256,
        nsig_max * C.TGT * 128 + 2 * (DM // 128) * 128 + 2 * C.TGT * 128
        + 2 * 2 * C.TGT * 128 + 256))
    arena = sb("arena", [128, AR], BF16)
    A2 = max(8 * DM, 2 * C.KC * C.FGMAX)
    arena2 = sb("arena2", [128, A2], BF16)
    a2f = arena2[:, :].bitcast(F32)

    arena_f32 = arena[:, :].bitcast(F32)
    arena_i32 = arena[:, :].bitcast(I32)

    def av_f32(col_bf, n):
        assert col_bf % 2 == 0
        return arena_f32[:, col_bf // 2:col_bf // 2 + n]

    def av_i32(col_bf, n):
        assert col_bf % 2 == 0
        return arena_i32[:, col_bf // 2:col_bf // 2 + n]

    def av_bf(col_bf, n):
        return arena[:, col_bf:col_bf + n]

    o = 0
    qcodes = [av_i32(o, C.QKMAX), av_i32(o + 2 * C.QKMAX, C.QKMAX)]
    o += 4 * C.QKMAX
    qv = av_f32(o, C.QKMAX); o += 2 * C.QKMAX
    qu = [av_f32(o, C.QKMAX), av_f32(o + 2 * C.QKMAX, C.QKMAX)]
    o += 4 * C.QKMAX
    qw = [av_bf(o, C.QKMAX), av_bf(o + C.QKMAX, C.QKMAX)]
    o += 2 * C.QKMAX
    qam = [av_f32(o, C.QKMAX // 64), av_f32(o + 2 * (C.QKMAX // 64), C.QKMAX // 64)]

    o = TAIL
    ry1 = [av_bf(o + i * C.FGMAX, C.FGMAX) for i in range(C.TC)]
    o += C.TC * C.FGMAX
    x3st = [av_bf(o, C.FGMAX), av_bf(o + C.FGMAX, C.FGMAX)]
    o += 2 * C.FGMAX


    o = TAIL
    a_sb = av_bf(o, nsig_max * C.TGT * 128)
    o += nsig_max * C.TGT * 128
    w1t = [av_bf(o, sspan_max * 128), av_bf(o + sspan_max * 128, sspan_max * 128)]
    o += 2 * sspan_max * 128
    w2t = [av_bf(o, C.TGT * 128), av_bf(o + C.TGT * 128, C.TGT * 128)]
    o += 2 * C.TGT * 128
    yg_t = [av_f32(o, C.TGT * 128), av_f32(o + 2 * C.TGT * 128, C.TGT * 128)]

    eps_t = sb("eps", [128, 1], F32)
    red = [sb(f"red{i}", [128, 1], F32) for i in range(2)]
    mu_t = sb("mu", [128, 1], F32)
    m2_t = sb("m2", [128, 1], F32)
    var_t = sb("var", [128, 1], F32)
    lnv_t = sb("lnv", [128, 1], F32)
    rstd_t = sb("rstd", [128, 1], F32)
    nb_t = sb("nb", [128, 1], F32)
    ones_bf = sb("ones_bf", [1, 128], BF16)
    b2row = sb("b2row", [1, C.TGT * 128], F32)
    b2row_bf = sb("b2row_bf", [1, C.TGT * 128], BF16)
    la_d = sb("la_d", [128, C.FC * R], BF16)
    la_g = sb("la_g", [128, C.KC * R], BF16)
    la_u = sb("la_u", [128, C.KC * R], BF16)
    t1_g = sb("t1_g", [R, T], BF16)
    t1_u = sb("t1_u", [R, T], BF16)
    lb_f32 = sb("lb_f32", [R, max(C.FGMAX, C.OGMAX)], F32)
    lb_g = sb("lb_g", [R, C.FGMAX], BF16)
    lb_u = sb("lb_u", [R, C.FGMAX], BF16)
    lb_d = sb("lb_d", [R, C.OGMAX], BF16)
    t1_d = sb("t1_d", [R, C.TH], BF16)
    dwt = [sb(f"dwt{i}", [128, C.OGMAX], BF16) for i in range(4)]
    xr_t = [sb(f"xr{i}", [128, max(C.OGMAX, C.TGT * 128)], F32) for i in range(4)]
    nx_t = [sb(f"nx{i}", [128, max(C.OGMAX, C.TGT * 128)], F32) for i in range(4)]
    mk_t = [sb(f"mk{i}", [128, 128], F32) for i in range(2)]

    pb = [es.enter_context(nc.psum_tensor(f"pb{i}_{tag}", [128, 512], F32))
          for i in range(8)]

    BS, A0 = _nf4_chain_coeffs()

    def gstrip(kc):
        return arena2[:, kc * C.FGMAX:(kc + 1) * C.FGMAX]

    def ustrip(kc):
        return arena2[:, (C.KC + kc) * C.FGMAX:(C.KC + kc + 1) * C.FGMAX]

    def x1fm(kc):
        return arena[:, kc * T:(kc + 1) * T]

    def x3fm(jc):
        return arena[:, jc * C.TH:(jc + 1) * C.TH]

    b.eng("ve", lambda e: e.memset(eps_t[:, :], 1e-5))
    b.eng("ve", lambda e: e.memset(ones_bf[:, :], 1.0))
    b.dma("gp", w1b, w1, sem="wcast")
    wcast_all = b.dma("gp", w2b, w2, sem="wcast")
    la_stage = a2f[:, :max(C.KC, C.FC) * R]
    t_ = b.dma("sc", la_stage[:, :C.KC * R].rearrange("p (c r) -> p c r", r=R),
               gla.rearrange("(c p) r -> p c r", p=128), sem="lstage")
    lag_tok = b.eng("ve", lambda e: e.tensor_copy(out=la_g[:, :], in_=la_stage[:, :C.KC * R]),
                    deps=[t_])
    t_ = b.dma("sc", la_stage[:, :C.KC * R].rearrange("p (c r) -> p c r", r=R),
               ula.rearrange("(c p) r -> p c r", p=128), deps=[lag_tok], sem="lstage")
    lau_tok = b.eng("ve", lambda e: e.tensor_copy(out=la_u[:, :], in_=la_stage[:, :C.KC * R]),
                    deps=[t_])
    t_ = b.dma("sc", la_stage[:, :C.FC * R].rearrange("p (c r) -> p c r", r=R),
               dla.rearrange("(c p) r -> p c r", p=128), deps=[lau_tok], sem="lstage")
    lad_tok = b.eng("ve", lambda e: e.tensor_copy(out=la_d[:, :], in_=la_stage[:, :C.FC * R]),
                    deps=[t_])

    def emit_ln(src_ap, init_dep=None):
        toks = []
        prev_ld = [init_dep, init_dep]
        prev_wr = [None, None]
        for tc in range(C.TC):
            p = tc % 2
            xt = a2f[:, p * DM:(p + 1) * DM]
            sq = a2f[:, 2 * DM:3 * DM]
            xb = arena2[:, 6 * DM + p * DM:6 * DM + (p + 1) * DM]
            ld = b.dma("sc", xt, src_ap[tc * 128:(tc + 1) * 128, :], deps=[prev_ld[p]], sem=f"lnx{p}")
            b.eng("ve", (lambda e, xt=xt: e.tensor_reduce(
                out=red[0][:, :], in_=xt, axis=mybir.AxisListType.X, op=ALU.add)),
                deps=[ld])
            b.eng("ve", (lambda e, xt=xt, sq=sq: e.tensor_tensor(
                out=sq, in0=xt, in1=xt, op=ALU.mult)), deps=[ld])
            b.eng("ve", (lambda e, sq=sq: e.tensor_reduce(
                out=red[1][:, :], in_=sq, axis=mybir.AxisListType.X, op=ALU.add)))
            b.eng("ve", lambda e: e.tensor_single_scalar(
                out=mu_t[:, :], in_=red[0][:, :], scalar=1.0 / DM, op=ALU.mult))
            b.eng("ve", lambda e: e.tensor_tensor(
                out=m2_t[:, :], in0=mu_t[:, :], in1=mu_t[:, :], op=ALU.mult))
            tvar = b.eng("ve", lambda e: e.scalar_tensor_tensor(
                out=var_t[:, :], in0=red[1][:, :], scalar=1.0 / DM, in1=m2_t[:, :],
                op0=ALU.mult, op1=ALU.subtract))
            b.eng("sc", lambda e: e.activation(
                out=lnv_t[:, :], in_=var_t[:, :], func=AF.Ln, bias=eps_t[:, :],
                scale=1.0), deps=[tvar])
            trs = b.eng("sc", lambda e: e.activation(
                out=rstd_t[:, :], in_=lnv_t[:, :], func=AF.Exp, scale=-0.5))
            b.eng("ve", lambda e: e.tensor_tensor(
                out=nb_t[:, :], in0=mu_t[:, :], in1=rstd_t[:, :], op=ALU.mult),
                deps=[trs])
            tnb = b.eng("ve", lambda e: e.tensor_single_scalar(
                out=nb_t[:, :], in_=nb_t[:, :], scalar=-1.0, op=ALU.mult))
            tact = b.eng("sc", (lambda e, xt=xt, xb=xb: e.activation(
                out=xb, in_=xt, func=AF.Identity, bias=nb_t[:, :],
                scale=rstd_t[:, :])), deps=[tnb, prev_wr[p]])
            prev_ld[p] = tact
            wr = b.dma("sc", x1tm[tc * 128:(tc + 1) * 128, :], xb, deps=[tact], sem=f"x1tm{p}")
            prev_wr[p] = wr
            toks.append(wr)
        return toks

    def emit_dequant(codes_ap, am_ap, cache_ap, nrow_chunks, ktiles, state):
        for rc in range(nrow_chunks):
            for (ks, kn) in ktiles:
                p = state["ti"] % 2
                nb = kn // 64
                ld_c = b.dma("sp", qcodes[p][:, :kn],
                             codes_ap[rc * 128:(rc + 1) * 128, ks:ks + kn],
                             deps=[state["code_rd"][p]], sem=f"qc{p}")
                ld_a = b.dma("sc", qam[p][:, :nb],
                             am_ap[rc * 128:(rc + 1) * 128, ks // 64:ks // 64 + nb],
                             deps=[state["w_wr"][p]], sem=f"qa{p}")
                tv = b.eng("ve", (lambda e, p=p, kn=kn: e.tensor_scalar(
                    out=qv[:, :kn], in0=qcodes[p][:, :kn], scalar1=VSCALE,
                    scalar2=VSHIFT, op0=ALU.mult, op1=ALU.add)), deps=[ld_c])
                state["code_rd"][p] = tv
                b.eng("ve", (lambda e, kn=kn: e.tensor_single_scalar(
                    out=qu[0][:, :kn], in_=qv[:, :kn], scalar=BS[0], op=ALU.mult)))
                ping = 0
                for bk in BS[1:]:
                    ping ^= 1
                    b.eng("ve", (lambda e, bk=bk, ping=ping, kn=kn:
                                 e.scalar_tensor_tensor(
                                     out=qu[ping][:, :kn], in0=qu[ping ^ 1][:, :kn],
                                     scalar=bk, in1=qv[:, :kn],
                                     op0=ALU.add, op1=ALU.mult)))
                tw = b.eng("ve", (lambda e, p=p, ping=ping, kn=kn, nb=nb:
                                  e.scalar_tensor_tensor(
                                      out=qw[p][:, :kn].rearrange("p (a c) -> p a c", a=nb),
                                      in0=qu[ping][:, :kn].rearrange("p (a c) -> p a c", a=nb),
                                      scalar=A0,
                                      in1=qam[p][:, :nb, None].broadcast_to([128, nb, 64]),
                                      op0=ALU.add, op1=ALU.mult)),
                           deps=[ld_a, state["w_wr"][p]])
                wr = b.dma("sc", cache_ap[rc * 128:(rc + 1) * 128, ks:ks + kn],
                           qw[p][:, :kn], deps=[tw], sem=f"qw{p}")
                state["w_wr"][p] = wr
                state["ti"] += 1

    dq_state = {"ti": 0, "code_rd": [None, None], "w_wr": [None, None]}
    groups = [list(range(C.NCORES))]
    ln1 = emit_ln(x_in, init_dep=lad_tok)
    DQS = 12 * C.QKMAX + 2 * (C.QKMAX // 64) + 256
    KC_EARLY = [kc for kc in range(C.KC) if kc * T >= DQS]
    emit_dequant(gc, ga, shard_g, C.GU_SH_CH, C.QK_GU, dq_state)
    ag_g = b.collective("AllGather", ALU.bypass, groups, [shard_g], [gcache],
                        deps=[dq_state["w_wr"][0], dq_state["w_wr"][1]], sem="agg")
    for kc in KC_EARLY:
        b.dma("sp", x1fm(kc), x1tm[:, kc * 128:(kc + 1) * 128],
              deps=ln1[-2:], transpose=True, sem="xfm")
    emit_dequant(uc, ua, shard_u, C.GU_SH_CH, C.QK_GU, dq_state)
    ag_u = b.collective("AllGather", ALU.bypass, groups, [shard_u], [ucache],
                        deps=[dq_state["w_wr"][0], dq_state["w_wr"][1]], sem="agu")
    emit_dequant(dc, da, shard_d, C.D_SH_CH, C.QK_D, dq_state)
    ag_d = b.collective("AllGather", ALU.bypass, groups, [shard_d], [dcache],
                        deps=[dq_state["w_wr"][0], dq_state["w_wr"][1]], sem="agd")
    fm_last = None
    for kc in range(C.KC):
        if kc in KC_EARLY:
            continue
        fm_last = b.dma("sp", x1fm(kc), x1tm[:, kc * 128:(kc + 1) * 128],
                        deps=ln1[-2:] + [dq_state["w_wr"][0], dq_state["w_wr"][1]],
                        transpose=True, sem="xfm")

    def emit_ffn(app, resid_src, resid_dst, x1_toks, fm_pre=None,
                 strip_dep=(None, None), dwt_dep=None):
        if fm_pre is not None:
            fm_all = fm_pre
        else:
            fm_all = None
            for kc in range(C.KC):
                fm_all = b.dma("sp", x1fm(kc), x1tm[:, kc * 128:(kc + 1) * 128],
                               deps=x1_toks[-2:], transpose=True, sem="xfm")
        t1_prev = None
        for (la, la_tok, t1sb) in ((la_g, lag_tok, t1_g), (la_u, lau_tok, t1_u)):
            for (ts, tn) in C.TPIECES:
                mm = None
                for kc in range(C.KC):
                    mm = b.eng("pe", (lambda e, kc=kc, ts=ts, tn=tn, la=la:
                                      e.matmul(pb[4][:R, :tn],
                                               la[:, kc * R:(kc + 1) * R],
                                               x1fm(kc)[:, ts:ts + tn],
                                               start=(kc == 0), stop=(kc == C.KC - 1))),
                               deps=[fm_all, la_tok] + ([t1_prev] if kc == 0 else []))
                t1_prev = b.eng("ve", (lambda e, t1sb=t1sb, ts=ts, tn=tn: e.tensor_copy(
                    out=t1sb[:, ts:ts + tn], in_=pb[4][:R, :tn])), deps=[mm])
        t1done = t1_prev

        def load_strip(which, f0, fn, dep):
            cache = gcache if which == "g" else ucache
            stf = gstrip if which == "g" else ustrip
            tok = None
            for kc in range(C.KC):
                tok = b.dma("sp", stf(kc)[:, :fn],
                            cache[f0:f0 + fn, kc * 128:(kc + 1) * 128],
                            deps=(dep if isinstance(dep, list) else [dep]),
                            transpose=True, sem=f"st{which}")
            return (tok,)

        lb_busy = [None]

        def load_lb(src, dst, c0, cn, war_dep):
            l1 = b.dma("sc", lb_f32[:, :cn], src[:, c0:c0 + cn], deps=[lb_busy[0]], sem="lb")
            l2 = b.eng("ve", (lambda e, dst=dst, cn=cn: e.tensor_copy(
                out=dst[:, :cn], in_=lb_f32[:, :cn])), deps=[l1, war_dep])
            lb_busy[0] = l2
            return l2

        war = {}
        x3_last = {}
        gstrip_all = load_strip("g", *C.FGROUPS[0], dep=strip_dep[0])
        ustrip_all = load_strip("u", *C.FGROUPS[0], dep=strip_dep[1])
        for gi, (f0, fn) in enumerate(C.FGROUPS):
            lbg = load_lb(glb, lb_g, f0, fn, war.get("lbg"))
            lbu = load_lb(ulb, lb_u, f0, fn, war.get("lbu"))
            relu_toks = []
            y1_last = None
            for tc in range(C.TC):
                bank = pb[tc % 2]
                for kc in range(C.KC):
                    b.eng("pe", (lambda e, kc=kc, tc=tc, fn=fn, bank=bank:
                                 e.matmul(bank[:, :fn],
                                          x1fm(kc)[:, tc * 128:(tc + 1) * 128],
                                          gstrip(kc)[:, :fn],
                                          start=(kc == 0), stop=False)),
                          deps=(list(gstrip_all) + [fm_all, t1done,
                                 war.get(("y1", tc % 2))] if kc == 0 else []))
                mm = b.eng("pe", (lambda e, tc=tc, fn=fn, bank=bank:
                                  e.matmul(bank[:, :fn],
                                           t1_g[:, tc * 128:(tc + 1) * 128],
                                           lb_g[:R, :fn], start=False, stop=True)),
                           deps=[lbg, t1done])
                war["lbg"] = mm
                y1_last = mm
                rt = b.eng("sc", (lambda e, tc=tc, fn=fn, bank=bank: e.activation(
                    out=ry1[tc][:, :fn], in_=bank[:, :fn], func=AF.Relu)),
                    deps=[mm, war.get(("ry", tc))])
                relu_toks.append(rt)
                war[("y1", tc % 2)] = rt
            if gi + 1 < len(C.FGROUPS):
                nf0, nfn = C.FGROUPS[gi + 1]
                gstrip_next = load_strip("g", nf0, nfn, y1_last)
            y2_last = None
            for tc in range(C.TC):
                bank = pb[2 + tc % 2]
                for kc in range(C.KC):
                    b.eng("pe", (lambda e, kc=kc, tc=tc, fn=fn, bank=bank:
                                 e.matmul(bank[:, :fn],
                                          x1fm(kc)[:, tc * 128:(tc + 1) * 128],
                                          ustrip(kc)[:, :fn],
                                          start=(kc == 0), stop=False)),
                          deps=(list(ustrip_all) + [t1done,
                                 war.get(("y2", tc % 2))] if kc == 0 else []))
                mm = b.eng("pe", (lambda e, tc=tc, fn=fn, bank=bank:
                                  e.matmul(bank[:, :fn],
                                           t1_u[:, tc * 128:(tc + 1) * 128],
                                           lb_u[:R, :fn], start=False, stop=True)),
                           deps=[lbu, t1done])
                war["lbu"] = mm
                y2_last = mm
                xp = tc % 2
                x3t = b.eng("ve", (lambda e, tc=tc, fn=fn, bank=bank, xp=xp:
                                   e.tensor_tensor(out=x3st[xp][:, :fn],
                                                   in0=ry1[tc][:, :fn],
                                                   in1=bank[:, :fn], op=ALU.mult)),
                            deps=[relu_toks[tc], mm, war.get(("x3", xp))])
                war[("ry", tc)] = x3t
                war[("y2", tc % 2)] = x3t
                wr = b.dma("gp", x3tm[tc * 128:(tc + 1) * 128, f0:f0 + fn],
                           x3st[xp][:, :fn], deps=[x3t], sem=f"x3tm{xp}")
                war[("x3", xp)] = wr
                x3_last[xp] = wr
            if gi + 1 < len(C.FGROUPS):
                ustrip_all = load_strip("u", nf0, nfn, y2_last)
                gstrip_all = gstrip_next

        out_toks = []
        rw_last = {}
        dwar = {}
        for h in range(2):
            hrow = h * C.TH
            x3f_all = None
            for jc in range(C.FC):
                x3f_all = b.dma("sp", x3fm(jc),
                                x3tm[hrow:hrow + C.TH, jc * 128:(jc + 1) * 128],
                                deps=list(x3_last.values()) + list(rw_last.values()),
                                transpose=True, sem="x3fm")
            mm = None
            for jc in range(C.FC):
                mm = b.eng("pe", (lambda e, jc=jc: e.matmul(
                    pb[4][:R, :C.TH], la_d[:, jc * R:(jc + 1) * R], x3fm(jc)[:, :],
                    start=(jc == 0), stop=(jc == C.FC - 1))),
                    deps=[x3f_all, lad_tok] + ([dwar.get("t1d")] if jc == 0 else []))
            t1d_ev = b.eng("ve", lambda e: e.tensor_copy(out=t1_d[:, :],
                                                         in_=pb[4][:R, :C.TH]),
                           deps=[mm])
            dwar["t1d"] = t1d_ev
            for oi, (o0, on) in enumerate(C.OGROUPS):
                lbd = load_lb(dlb, lb_d, o0, on, dwar.get("lbd"))
                banks = [pb[(oi % 2) * 4 + i] for i in range(C.THC)]
                last_mm = [None] * C.THC
                for jc in range(C.FC):
                    p = jc % 4
                    ldw = b.dma("sp", dwt[p][:, :on],
                                dcache[o0:o0 + on, jc * 128:(jc + 1) * 128],
                                deps=[dwar.get(("dw", p)), dwt_dep],
                                transpose=True, sem=f"dw{p}")
                    for tcl in range(C.THC):
                        last_mm[tcl] = b.eng(
                            "pe", (lambda e, jc=jc, tcl=tcl, on=on, p=p, bank=banks[tcl]:
                                   e.matmul(bank[:, :on],
                                            x3fm(jc)[:, tcl * 128:(tcl + 1) * 128],
                                            dwt[p][:, :on],
                                            start=(jc == 0), stop=False)),
                            deps=[ldw, x3f_all, t1d_ev]
                            + ([dwar.get(("bk", oi % 2, tcl))] if jc == 0 else []))
                    dwar[("dw", p)] = last_mm[C.THC - 1]
                for tcl in range(C.THC):
                    lm = b.eng("pe", (lambda e, tcl=tcl, on=on, bank=banks[tcl]:
                                      e.matmul(bank[:, :on],
                                               t1_d[:, tcl * 128:(tcl + 1) * 128],
                                               lb_d[:R, :on], start=False, stop=True)),
                               deps=[lbd, t1d_ev])
                    dwar["lbd"] = lm
                    xp = (oi * C.THC + tcl) % 4
                    ldx = b.dma("sc", xr_t[xp][:, :on],
                                resid_src[hrow + tcl * 128:hrow + (tcl + 1) * 128,
                                          o0:o0 + on],
                                deps=[dwar.get(("xr", xp))], sem=f"xr{xp}")
                    ev = b.eng("ve", (lambda e, on=on, bank=banks[tcl], xp=xp:
                                      e.scalar_tensor_tensor(
                                          out=nx_t[xp][:, :on], in0=bank[:, :on],
                                          scalar=0.5, in1=xr_t[xp][:, :on],
                                          op0=ALU.mult, op1=ALU.add)),
                               deps=[lm, ldx, dwar.get(("nx", xp))])
                    dwar[("bk", oi % 2, tcl)] = ev
                    dwar[("xr", xp)] = ev
                    wr = b.dma("gp", resid_dst[hrow + tcl * 128:hrow + (tcl + 1) * 128,
                                               o0:o0 + on],
                               nx_t[xp][:, :on], deps=[ev], sem=f"rw{xp}")
                    dwar[("nx", xp)] = wr
                    rw_last[xp] = wr
                    out_toks.append(wr)
        return list(rw_last.values())

    def emit_attn(z_toks, resid_src, resid_dst):
        zf_all = None
        for kc in range(C.KC):
            zf_all = b.dma("sp", x1fm(kc), x1tm[:, kc * 128:(kc + 1) * 128],
                           deps=z_toks[-2:], transpose=True, sem="xfm")
        tile_idx = {st: i for i, st in enumerate(band_tiles)}
        out_toks = []
        prev = {}
        for g0 in range(0, C.NT, C.TGT):
            gts, sigs = group_sigs[g0]
            gcols = len(gts) * 128
            smin = sigs[0]
            sspan = sigs[-1] - smin + 1
            cons_last = {}
            for kc in range(C.KC):
                p = kc % 2
                lw1 = b.dma("sp", w1t[p][:, :sspan * 128],
                            w1b[kc * 128:(kc + 1) * 128, smin * 128:(smin + sspan) * 128],
                            deps=[wcast_all, prev.get(("w1t", p))], sem=f"w1t{p}")
                lw2 = b.dma("sp", w2t[p][:, :gcols],
                            w2b[g0 * 128:g0 * 128 + gcols, kc * 128:(kc + 1) * 128],
                            deps=[wcast_all, prev.get(("w2t", p))], transpose=True, sem=f"w2t{p}")
                for si, sg in enumerate(sigs):
                    mm = b.eng("pe", (lambda e, si=si, sg=sg, p=p, gcols=gcols, kc=kc,
                                      smin=smin:
                                      e.matmul(pb[si][:, :gcols],
                                               w1t[p][:, (sg - smin) * 128:(sg - smin + 1) * 128],
                                               w2t[p][:, :gcols],
                                               start=(kc == 0), stop=(kc == C.KC - 1))),
                               deps=[lw1, lw2]
                               + ([prev.get(("cons", si)), prev.get(("y", 0)),
                                   prev.get(("y", 1))] if kc == 0 else []))
                    cons_last[si] = mm
                prev[("w1t", p)] = cons_last[len(sigs) - 1]
                prev[("w2t", p)] = cons_last[len(sigs) - 1]
            ev_toks = {}
            mi = 0
            for si, sg in enumerate(sigs):
                taus = sorted(t for t in gts if (sg, t) in tile_idx)
                last_ev = None
                for t in taus:
                    mp = mi % 2
                    lm = b.dma("sc", mk_t[mp][:, :], maskb[tile_idx[(sg, t)], :, :],
                               deps=[prev.get(("mk", mp))], sem=f"mk{mp}")
                    col = (si * C.TGT + (t - g0)) * 128
                    evt = b.eng("ve", (lambda e, si=si, t=t, g0=g0, col=col, mp=mp:
                                       e.tensor_tensor(
                                           out=a_sb[:, col:col + 128],
                                           in0=pb[si][:, (t - g0) * 128:(t - g0 + 1) * 128],
                                           in1=mk_t[mp][:, :], op=ALU.mult)),
                                deps=[cons_last[si], lm, prev.get(("asb", si, t))])
                    prev[("mk", mp)] = evt
                    prev[("asb", si, t)] = evt
                    ev_toks[(sg, t)] = evt
                    last_ev = evt
                    mi += 1
                prev[("cons", si)] = last_ev
            lb2 = b.dma("sc", b2row[:1, :gcols],
                        b2[g0 * 128:g0 * 128 + gcols].rearrange("(a n) -> a n", a=1),
                        deps=[prev.get("b2"), prev.get("b2mm")], sem="b2l")
            cb2 = b.eng("ve", (lambda e, gcols=gcols: e.tensor_copy(
                out=b2row_bf[:1, :gcols], in_=b2row[:1, :gcols])), deps=[lb2])
            prev["b2"] = cb2
            all_ev = list(ev_toks.values())
            for tc in range(C.TC):
                ybi = 6 + (tc % 2) if nsig_max <= 6 else (tc % 2)
                yb = pb[ybi]
                plan = []
                for sg in sigs:
                    taus = sorted(t for t in gts if (sg, t) in tile_idx)
                    if taus:
                        plan.append((sg, taus))
                dep0 = [cb2, prev.get(("y", tc % 2))] + all_ev
                bmm = b.eng("pe", (lambda e, gcols=gcols, yb=yb: e.matmul(
                    yb[:, :gcols], ones_bf[:1, :], b2row_bf[:1, :gcols],
                    start=True, stop=False)), deps=dep0)
                prev["b2mm"] = bmm
                for pi, (sg, taus) in enumerate(plan):
                    c0 = (taus[0] - g0) * 128
                    c1 = (taus[-1] - g0 + 1) * 128
                    si = sigs.index(sg)
                    acol = (si * C.TGT + (taus[0] - g0)) * 128
                    last = (pi == len(plan) - 1)
                    mm = b.eng("pe", (lambda e, sg=sg, c0=c0, c1=c1, tc=tc,
                                      acol=acol, yb=yb, last=last:
                                      e.matmul(yb[:, c0:c1],
                                               x1fm(sg)[:, tc * 128:(tc + 1) * 128],
                                               a_sb[:, acol:acol + (c1 - c0)],
                                               start=False, stop=last)),
                               deps=[zf_all])
                yp = tc % 2
                gl = b.eng("sc", (lambda e, gcols=gcols, yb=yb, yp=yp: e.activation(
                    out=yg_t[yp][:, :gcols], in_=yb[:, :gcols], func=AF.Gelu)),
                    deps=[mm, prev.get(("yg", yp))])
                xp = tc % 4
                ldx = b.dma("sc", xr_t[xp][:, :gcols],
                            resid_src[tc * 128:(tc + 1) * 128, g0 * 128:g0 * 128 + gcols],
                            deps=[prev.get(("axr", xp))], sem=f"xr{xp}")
                ad = b.eng("ve", (lambda e, gcols=gcols, xp=xp, yp=yp: e.tensor_tensor(
                    out=nx_t[xp][:, :gcols], in0=yg_t[yp][:, :gcols],
                    in1=xr_t[xp][:, :gcols], op=ALU.add)),
                    deps=[gl, ldx, prev.get(("anx", xp))])
                prev[("y", tc % 2)] = ad
                prev[("yg", yp)] = ad
                wr = b.dma("gp", resid_dst[tc * 128:(tc + 1) * 128,
                                           g0 * 128:g0 * 128 + gcols],
                           nx_t[xp][:, :gcols], deps=[ad], sem=f"rw{xp}")
                prev[("axr", xp)] = ad
                prev[("anx", xp)] = wr
                out_toks.append(wr)
        return out_toks

    ffn1 = emit_ffn(1, x_in, resid2, ln1, fm_pre=fm_last,
                    strip_dep=([ag_g] + ln1[-2:], [ag_u] + ln1[-2:]), dwt_dep=ag_d)
    b.barrier()
    ln2 = emit_ln(resid2)
    b.barrier()
    attn = emit_attn(ln2, resid2, resid3)
    b.barrier()
    ln3 = emit_ln(resid3)
    b.barrier()
    ffn2 = emit_ffn(2, resid3, out, ln3)
    b.barrier()
    b.eng("gp", lambda e: e.engine_nop(), deps=ffn2)

    b.finalize()
    return nc


def kernel(L0_attn_x, L0_attn_w1, L0_attn_w2, L0_attn_b2, L0_attn_sparse_mask,
           L0_ffn_g_codes, L0_ffn_g_absmax, L0_ffn_g_lora_a, L0_ffn_g_lora_b,
           L0_ffn_u_codes, L0_ffn_u_absmax, L0_ffn_u_lora_a, L0_ffn_u_lora_b,
           L0_ffn_d_codes, L0_ffn_d_absmax, L0_ffn_d_lora_a, L0_ffn_d_lora_b):
    DM = 4096
    DFF = 11008
    Tfull = int(np.prod(L0_attn_x.shape[:-1]))
    T = Tfull // NCORES
    cfg = Cfg(DM, DFF, T)
    band = band_tiles_from_mask_shape(DM)

    x = np.ascontiguousarray(np.asarray(L0_attn_x, dtype=np.float32).reshape(Tfull, DM))
    mask = np.asarray(L0_attn_sparse_mask, dtype=np.float32)
    maskb = np.ascontiguousarray(
        np.stack([mask[s * 128:(s + 1) * 128, t * 128:(t + 1) * 128] for (s, t) in band]))

    def shard_rows(a, rows_per_core):
        a = np.asarray(a)
        total = rows_per_core * NCORES
        if a.shape[0] < total:
            pad = np.zeros((total - a.shape[0],) + a.shape[1:], dtype=a.dtype)
            a = np.concatenate([a, pad], axis=0)
        return [np.ascontiguousarray(a[c * rows_per_core:(c + 1) * rows_per_core])
                for c in range(NCORES)]

    GUR = cfg.GU_SH_CH * 128
    DR = cfg.D_SH_CH * 128
    gc_sh = shard_rows(np.asarray(L0_ffn_g_codes, dtype=np.int32), GUR)
    ga_sh = shard_rows(np.asarray(L0_ffn_g_absmax, dtype=np.float32), GUR)
    uc_sh = shard_rows(np.asarray(L0_ffn_u_codes, dtype=np.int32), GUR)
    ua_sh = shard_rows(np.asarray(L0_ffn_u_absmax, dtype=np.float32), GUR)
    dc_sh = shard_rows(np.asarray(L0_ffn_d_codes, dtype=np.int32), DR)
    da_sh = shard_rows(np.asarray(L0_ffn_d_absmax, dtype=np.float32), DR)

    common = {
        "w1": np.ascontiguousarray(np.asarray(L0_attn_w1, dtype=np.float32)),
        "w2": np.ascontiguousarray(np.asarray(L0_attn_w2, dtype=np.float32)),
        "maskb": maskb,
        "b2": np.ascontiguousarray(np.asarray(L0_attn_b2, dtype=np.float32)),

        "gla": np.ascontiguousarray(np.asarray(L0_ffn_g_lora_a, dtype=np.float32)),
        "glb": np.ascontiguousarray(np.asarray(L0_ffn_g_lora_b, dtype=np.float32)),
        "ula": np.ascontiguousarray(np.asarray(L0_ffn_u_lora_a, dtype=np.float32)),
        "ulb": np.ascontiguousarray(np.asarray(L0_ffn_u_lora_b, dtype=np.float32)),
        "dla": np.ascontiguousarray(np.asarray(L0_ffn_d_lora_a, dtype=np.float32)),
        "dlb": np.ascontiguousarray(np.asarray(L0_ffn_d_lora_b, dtype=np.float32)),
    }
    in_maps = []
    for c in range(NCORES):
        m = dict(common)
        m["x"] = np.ascontiguousarray(x[c * T:(c + 1) * T])
        m["gc"], m["ga"] = gc_sh[c], ga_sh[c]
        m["uc"], m["ua"] = uc_sh[c], ua_sh[c]
        m["dc"], m["da"] = dc_sh[c], da_sh[c]
        in_maps.append(m)

    nc = build_program(cfg, band)
    res = run_bass_kernel_spmd(nc, in_maps, core_ids=list(range(NCORES)))
    shards = [np.asarray(r["out"]).reshape(T, DM) for r in res.results]
    return np.concatenate(shards, axis=0).reshape(L0_attn_x.shape).astype(np.float32)


# version 31
# speedup vs baseline: 1.0499x; 1.0499x over previous
import contextlib
import numpy as np

import concourse.bass as bass
import concourse.mybir as mybir
from concourse.bass_utils import run_bass_kernel_spmd
from concourse.dve_op import DveOp
from concourse import dve_ops as _dve_ops
from concourse.dve_spec import Spec, Src0, Src1, C0, C1, C2, lower as _dve_lower

F32 = mybir.dt.float32
BF16 = mybir.dt.bfloat16
I32 = mybir.dt.int32
AF = mybir.ActivationFunctionType
ALU = mybir.AluOpType

NCORES = 8

_NF4 = np.array([
    -1.0, -0.6961928009986877, -0.5250730514526367, -0.39491748809814453,
    -0.28444138169288635, -0.18477343022823334, -0.09105003625154495, 0.0,
    0.07958029955625534, 0.16093020141124725, 0.2461123019456863,
    0.33791524171829224, 0.44070982933044434, 0.5626170039176941,
    0.7229568362236023, 1.0], dtype=np.float64)

VSCALE = 0.25
VSHIFT = -7.5 * 0.25


def _nf4_chain_coeffs():
    c = np.arange(16, dtype=np.float64)
    v = c * VSCALE + VSHIFT
    V = np.vander(v, 16, increasing=True)
    p = np.linalg.solve(V, _NF4)
    for _ in range(3):
        p += np.linalg.solve(V, _NF4 - V @ p)
    a0 = float(p[0])
    bs = [float(x) for x in p[1:][::-1]]
    return bs, a0


_DVE_REG = {}


def _reg_dve(name, body, perf_en, ref):
    if name in _DVE_REG:
        return _DVE_REG[name]
    spec = Spec(body=body)
    shas = {}
    pe = {}
    for ver in ("v3", "v4"):
        try:
            shas[ver] = _dve_lower(spec, ver=ver, perf_en=perf_en, name=name).sha(ver)
            pe[ver] = perf_en
        except Exception:
            try:
                shas[ver] = _dve_lower(spec, ver=ver, perf_en=False, name=name).sha(ver)
                pe[ver] = False
            except Exception:
                pass
    op = DveOp(name, spec, subdim=False, uops_sha=shas, perf_en=pe, reference=ref)
    _dve_ops.OPS.append(op)
    _dve_ops._SUB_OPCODE_FOR_NAME[name] = _dve_ops._CUSTOM_DVE_ROW_BASE + len(_DVE_REG)
    _dve_ops.CUSTOM_DVE_REFS[name] = ref
    if hasattr(_dve_ops, "CUSTOM_DVE_SPECS"):
        _dve_ops.CUSTOM_DVE_SPECS[name] = spec
    _DVE_REG[name] = op
    return op


def _nf4_dve_ops():
    d_init = _reg_dve(
        "NF4_H_INIT", (Src0 * C0 + C1) * Src0 + C2, True,
        lambda in0, in1, s0, s1, imm2: ((np.float32(s0) * in0 + np.float32(s1))
                                        * in0 + np.float32(imm2)).astype(np.float32))
    d_step = _reg_dve(
        "NF4_H_STEP", ((Src1 * Src0 + C0) * Src0 + C1) * Src0 + C2, False,
        lambda in0, in1, s0, s1, imm2: (((in1 * in0 + np.float32(s0)) * in0
                                         + np.float32(s1)) * in0
                                        + np.float32(imm2)).astype(np.float32))
    d_tail = _reg_dve(
        "NF4_H_TAIL", (((Src1 * Src0 + C0) * Src0 + C1) * Src0 + C2) * Src0, False,
        lambda in0, in1, s0, s1, imm2: ((((in1 * in0 + np.float32(s0)) * in0
                                          + np.float32(s1)) * in0
                                         + np.float32(imm2)) * in0).astype(np.float32))
    return d_init, d_step, d_tail


def _chop(total, maxn, align=1):
    out = []
    s = 0
    while s < total:
        n = min(maxn, total - s)
        if n < total - s:
            n -= n % align
        out.append((s, n))
        s += n
    return out


class Cfg:
    def __init__(self, DM, DFF, T, FGMAX=512, OGMAX=512, QKMAX=2048, TGT=4, ncores=NCORES):
        self.DM, self.DFF, self.T = DM, DFF, T
        self.R = 16
        self.KC = DM // 128
        self.FC = DFF // 128
        self.TC = T // 128
        self.TH = T // 2
        self.THC = self.TH // 128
        self.FGROUPS = _chop(DFF, FGMAX)
        self.OGROUPS = _chop(DM, OGMAX)
        self.FGMAX = max(n for _, n in self.FGROUPS)
        self.OGMAX = max(n for _, n in self.OGROUPS)
        self.QK_GU = _chop(DM, QKMAX, 64)
        self.QK_D = _chop(DFF, QKMAX, 64)
        self.QKMAX = min(QKMAX, max(max(n for _, n in self.QK_GU),
                                    max(n for _, n in self.QK_D)))
        self.TGT = TGT
        self.NT = DM // 128
        self.TPIECES = _chop(T, 512)
        self.NCORES = ncores
        self.GU_SH_CH = -(-self.FC // ncores)
        self.D_SH_CH = -(-self.KC // ncores)
        self.GU_PAD_CH = self.GU_SH_CH * ncores
        self.D_PAD_CH = self.D_SH_CH * ncores


def band_tiles_from_mask_shape(DM):
    import math
    stride = int(math.sqrt(DM))
    m = np.zeros((DM, DM), dtype=bool)
    for k in range(3):
        ps = stride * k
        for i in range(0, DM, 6):
            m[ps + i:ps + i + 6, i:i + 6] = True
    m |= m.T
    nt = DM // 128
    return [(s, t) for s in range(nt) for t in range(nt)
            if m[s * 128:(s + 1) * 128, t * 128:(t + 1) * 128].any()]


_BUILD_N = [0]


class B:
    ENGINES = ("pe", "ve", "sc", "gp", "sp")

    def __init__(self, nc):
        self.nc = nc
        self.ops = {e: [] for e in self.ENGINES}
        self.count = {}
        self.sem_handles = {}
        self.bar = []
        self.prev_tok = {"ve": None, "sc": None}
        self.es = contextlib.ExitStack()
        for s in self.ENGINES:
            self._mksem(s)

    def _mksem(self, s):
        if s not in self.sem_handles:
            self.sem_handles[s] = self.es.enter_context(
                self.nc.semaphore(f"s{s}_{_BUILD_N[0]}"))
            self.count[s] = 0

    def _op(self, eng, fn, deps, sem, inc):
        deps = [d for d in deps if d is not None] + self.bar
        if eng in self.prev_tok and self.prev_tok[eng] is not None:
            deps = deps + [self.prev_tok[eng]]
        self.count[sem] += inc
        tok = (sem, self.count[sem])
        self.ops[eng].append((deps, fn, sem, inc))
        if eng in self.prev_tok:
            self.prev_tok[eng] = tok
        return tok

    def eng(self, eng, fn, deps=()):
        return self._op(eng, fn, deps, eng, 1)

    def dma(self, q, out_ap, in_ap, deps=(), transpose=False, sem=None):
        sem = f"D{q}_{sem or 'x'}"
        self._mksem(sem)

        def fn(e, out_ap=out_ap, in_ap=in_ap, transpose=transpose):
            if transpose:
                return e.dma_start(out=out_ap, in_=in_ap, transpose=True)
            return e.dma_start(out=out_ap, in_=in_ap)
        return self._op(q, fn, deps, sem, 16)

    def collective(self, kind, op, groups, ins, outs, deps=(), sem=None):
        sem = f"C_{sem or 'c'}"
        self._mksem(sem)

        def fn(e, kind=kind, op=op, groups=groups, ins=ins, outs=outs):
            return e.collective_compute(kind, op, replica_groups=groups,
                                        ins=ins, outs=outs)
        return self._op("gp", fn, deps, sem, 1)

    def barrier(self):
        self.bar = [(s, v) for s, v in self.count.items() if v > 0]

    def finalize(self):
        with self.nc.Block() as block:
            for eng in self.ENGINES:
                if not self.ops[eng]:
                    continue
                deco = {"pe": block.tensor, "ve": block.vector, "sc": block.scalar,
                        "gp": block.gpsimd, "sp": block.sync}[eng]
                ops = self.ops[eng]
                sems = self.sem_handles

                def section(e, ops=ops, sems=sems):
                    waited = {}
                    for deps, fn, sem, inc in ops:
                        for (ds, dv) in deps:
                            if waited.get(ds, 0) < dv:
                                e.wait_ge(sems[ds], dv)
                                waited[ds] = dv
                        fn(e).then_inc(sems[sem], inc)
                deco(section)
        self.es.close()


def build_program(cfg, band_tiles):
    _BUILD_N[0] += 1
    tag = f"k{_BUILD_N[0]}"
    nc = bass.Bass("TRN2", target_bir_lowering=False)
    C = cfg
    DM, DFF, T, R = C.DM, C.DFF, C.T, C.R

    def din(name, shape, dtype):
        return nc.dram_tensor(name, shape, dtype, kind="ExternalInput").ap()

    def dint(name, shape, dtype):
        return nc.dram_tensor(f"{name}_{tag}", shape, dtype, kind="Internal").ap()

    x_in = din("x", [T, DM], F32)
    w1 = din("w1", [DM, DM], F32)
    w2 = din("w2", [DM, DM], F32)
    maskb = din("maskb", [len(band_tiles), 128, 128], F32)
    b2 = din("b2", [DM], F32)
    GUR = C.GU_SH_CH * 128
    DR = C.D_SH_CH * 128
    gc, ga = din("gc", [GUR, DM], I32), din("ga", [GUR, DM // 64], F32)
    uc, ua = din("uc", [GUR, DM], I32), din("ua", [GUR, DM // 64], F32)
    dc, da = din("dc", [DR, DFF], I32), din("da", [DR, DFF // 64], F32)
    gla, glb = din("gla", [DM, R], F32), din("glb", [R, DFF], F32)
    ula, ulb = din("ula", [DM, R], F32), din("ulb", [R, DFF], F32)
    dla, dlb = din("dla", [DFF, R], F32), din("dlb", [R, DM], F32)
    out = nc.dram_tensor("out", [T, DM], F32, kind="ExternalOutput").ap()

    x1tm = dint("x1tm", [T, DM], BF16)
    x3tm = dint("x3tm", [T, DFF], BF16)
    resid2 = dint("resid2", [T, DM], F32)
    resid3 = dint("resid3", [T, DM], F32)
    shard_g = dint("shard_g", [GUR, DM], BF16)
    shard_u = dint("shard_u", [GUR, DM], BF16)
    shard_d = dint("shard_d", [DR, DFF], BF16)
    gcache = dint("gcache", [C.GU_PAD_CH * 128, DM], BF16)
    ucache = dint("ucache", [C.GU_PAD_CH * 128, DM], BF16)
    dcache = dint("dcache", [C.D_PAD_CH * 128, DFF], BF16)
    w1b = dint("w1b", [DM, DM], BF16)
    w2b = dint("w2b", [DM, DM], BF16)

    b = B(nc)
    es = b.es

    def sb(name, shape, dtype):
        return es.enter_context(nc.sbuf_tensor(f"{name}_{tag}", shape, dtype))

    band_set = set(band_tiles)
    group_sigs = {}
    for g0 in range(0, C.NT, C.TGT):
        gts = list(range(g0, min(g0 + C.TGT, C.NT)))
        sigs = sorted({s for (s, t) in band_set if t in gts})
        group_sigs[g0] = (gts, sigs)
    nsig_max = max(len(s) for _, s in group_sigs.values())
    sspan_max = max(s[-1] - s[0] + 1 for _, s in group_sigs.values())
    assert nsig_max <= 8, f"attn needs {nsig_max} psum banks"

    AR = max(C.KC * T,
             C.FC * C.TH,
             2 * 2 * C.QKMAX + 2 * C.QKMAX + 2 * 2 * C.QKMAX + 2 * C.QKMAX + 256)
    TAIL = C.KC * T
    AR = max(AR, TAIL + max(
        C.TC * C.FGMAX + 2 * C.FGMAX + 256,
        nsig_max * C.TGT * 128 + 2 * (DM // 128) * 128 + 2 * C.TGT * 128
        + 2 * 2 * C.TGT * 128 + 256))
    arena = sb("arena", [128, AR], BF16)
    A2 = max(8 * DM, 2 * C.KC * C.FGMAX)
    arena2 = sb("arena2", [128, A2], BF16)
    a2f = arena2[:, :].bitcast(F32)

    arena_f32 = arena[:, :].bitcast(F32)
    arena_i32 = arena[:, :].bitcast(I32)

    def av_f32(col_bf, n):
        assert col_bf % 2 == 0
        return arena_f32[:, col_bf // 2:col_bf // 2 + n]

    def av_i32(col_bf, n):
        assert col_bf % 2 == 0
        return arena_i32[:, col_bf // 2:col_bf // 2 + n]

    def av_bf(col_bf, n):
        return arena[:, col_bf:col_bf + n]

    o = 0
    qcodes = [av_i32(o, C.QKMAX), av_i32(o + 2 * C.QKMAX, C.QKMAX)]
    o += 4 * C.QKMAX
    qv = av_f32(o, C.QKMAX); o += 2 * C.QKMAX
    qu = [av_f32(o, C.QKMAX), av_f32(o + 2 * C.QKMAX, C.QKMAX)]
    o += 4 * C.QKMAX
    qw = [av_bf(o, C.QKMAX), av_bf(o + C.QKMAX, C.QKMAX)]
    o += 2 * C.QKMAX
    qam = [av_f32(o, C.QKMAX // 64), av_f32(o + 2 * (C.QKMAX // 64), C.QKMAX // 64)]

    o = TAIL
    ry1 = [av_bf(o + i * C.FGMAX, C.FGMAX) for i in range(C.TC)]
    o += C.TC * C.FGMAX
    x3st = [av_bf(o, C.FGMAX), av_bf(o + C.FGMAX, C.FGMAX)]
    o += 2 * C.FGMAX


    o = TAIL
    a_sb = av_bf(o, nsig_max * C.TGT * 128)
    o += nsig_max * C.TGT * 128
    w1t = [av_bf(o, sspan_max * 128), av_bf(o + sspan_max * 128, sspan_max * 128)]
    o += 2 * sspan_max * 128
    w2t = [av_bf(o, C.TGT * 128), av_bf(o + C.TGT * 128, C.TGT * 128)]
    o += 2 * C.TGT * 128
    yg_t = [av_f32(o, C.TGT * 128), av_f32(o + 2 * C.TGT * 128, C.TGT * 128)]

    eps_t = sb("eps", [128, 1], F32)
    red = [sb(f"red{i}", [128, 1], F32) for i in range(2)]
    mu_t = sb("mu", [128, 1], F32)
    m2_t = sb("m2", [128, 1], F32)
    var_t = sb("var", [128, 1], F32)
    lnv_t = sb("lnv", [128, 1], F32)
    rstd_t = sb("rstd", [128, 1], F32)
    nb_t = sb("nb", [128, 1], F32)
    ones_bf = sb("ones_bf", [1, 128], BF16)
    b2row = sb("b2row", [1, C.TGT * 128], F32)
    b2row_bf = sb("b2row_bf", [1, C.TGT * 128], BF16)
    la_d = sb("la_d", [128, C.FC * R], BF16)
    la_g = sb("la_g", [128, C.KC * R], BF16)
    la_u = sb("la_u", [128, C.KC * R], BF16)
    t1_g = sb("t1_g", [R, T], BF16)
    t1_u = sb("t1_u", [R, T], BF16)
    lb_f32 = sb("lb_f32", [R, max(C.FGMAX, C.OGMAX)], F32)
    lb_g = sb("lb_g", [R, C.FGMAX], BF16)
    lb_u = sb("lb_u", [R, C.FGMAX], BF16)
    lb_d = sb("lb_d", [R, C.OGMAX], BF16)
    t1_d = sb("t1_d", [R, C.TH], BF16)
    dwt = [sb(f"dwt{i}", [128, C.OGMAX], BF16) for i in range(4)]
    xr_t = [sb(f"xr{i}", [128, max(C.OGMAX, C.TGT * 128)], F32) for i in range(4)]
    nx_t = [sb(f"nx{i}", [128, max(C.OGMAX, C.TGT * 128)], F32) for i in range(4)]
    mk_t = [sb(f"mk{i}", [128, 128], F32) for i in range(2)]

    pb = [es.enter_context(nc.psum_tensor(f"pb{i}_{tag}", [128, 512], F32))
          for i in range(8)]

    BS, A0 = _nf4_chain_coeffs()

    def gstrip(kc):
        return arena2[:, kc * C.FGMAX:(kc + 1) * C.FGMAX]

    def ustrip(kc):
        return arena2[:, (C.KC + kc) * C.FGMAX:(C.KC + kc + 1) * C.FGMAX]

    def x1fm(kc):
        return arena[:, kc * T:(kc + 1) * T]

    def x3fm(jc):
        return arena[:, jc * C.TH:(jc + 1) * C.TH]

    b.eng("ve", lambda e: e.memset(eps_t[:, :], 1e-5))
    b.eng("ve", lambda e: e.memset(ones_bf[:, :], 1.0))
    b.dma("gp", w1b, w1, sem="wcast")
    wcast_all = b.dma("gp", w2b, w2, sem="wcast")
    la_stage = a2f[:, :max(C.KC, C.FC) * R]
    t_ = b.dma("sc", la_stage[:, :C.KC * R].rearrange("p (c r) -> p c r", r=R),
               gla.rearrange("(c p) r -> p c r", p=128), sem="lstage")
    lag_tok = b.eng("ve", lambda e: e.tensor_copy(out=la_g[:, :], in_=la_stage[:, :C.KC * R]),
                    deps=[t_])
    t_ = b.dma("sc", la_stage[:, :C.KC * R].rearrange("p (c r) -> p c r", r=R),
               ula.rearrange("(c p) r -> p c r", p=128), deps=[lag_tok], sem="lstage")
    lau_tok = b.eng("ve", lambda e: e.tensor_copy(out=la_u[:, :], in_=la_stage[:, :C.KC * R]),
                    deps=[t_])
    t_ = b.dma("sc", la_stage[:, :C.FC * R].rearrange("p (c r) -> p c r", r=R),
               dla.rearrange("(c p) r -> p c r", p=128), deps=[lau_tok], sem="lstage")
    lad_tok = b.eng("ve", lambda e: e.tensor_copy(out=la_d[:, :], in_=la_stage[:, :C.FC * R]),
                    deps=[t_])

    def emit_ln(src_ap, init_dep=None):
        toks = []
        prev_ld = [init_dep, init_dep]
        prev_wr = [None, None]
        for tc in range(C.TC):
            p = tc % 2
            xt = a2f[:, p * DM:(p + 1) * DM]
            sq = a2f[:, 2 * DM:3 * DM]
            xb = arena2[:, 6 * DM + p * DM:6 * DM + (p + 1) * DM]
            ld = b.dma("sc", xt, src_ap[tc * 128:(tc + 1) * 128, :], deps=[prev_ld[p]], sem=f"lnx{p}")
            b.eng("ve", (lambda e, xt=xt: e.tensor_reduce(
                out=red[0][:, :], in_=xt, axis=mybir.AxisListType.X, op=ALU.add)),
                deps=[ld])
            b.eng("ve", (lambda e, xt=xt, sq=sq: e.tensor_tensor(
                out=sq, in0=xt, in1=xt, op=ALU.mult)), deps=[ld])
            b.eng("ve", (lambda e, sq=sq: e.tensor_reduce(
                out=red[1][:, :], in_=sq, axis=mybir.AxisListType.X, op=ALU.add)))
            b.eng("ve", lambda e: e.tensor_single_scalar(
                out=mu_t[:, :], in_=red[0][:, :], scalar=1.0 / DM, op=ALU.mult))
            b.eng("ve", lambda e: e.tensor_tensor(
                out=m2_t[:, :], in0=mu_t[:, :], in1=mu_t[:, :], op=ALU.mult))
            tvar = b.eng("ve", lambda e: e.scalar_tensor_tensor(
                out=var_t[:, :], in0=red[1][:, :], scalar=1.0 / DM, in1=m2_t[:, :],
                op0=ALU.mult, op1=ALU.subtract))
            b.eng("sc", lambda e: e.activation(
                out=lnv_t[:, :], in_=var_t[:, :], func=AF.Ln, bias=eps_t[:, :],
                scale=1.0), deps=[tvar])
            trs = b.eng("sc", lambda e: e.activation(
                out=rstd_t[:, :], in_=lnv_t[:, :], func=AF.Exp, scale=-0.5))
            b.eng("ve", lambda e: e.tensor_tensor(
                out=nb_t[:, :], in0=mu_t[:, :], in1=rstd_t[:, :], op=ALU.mult),
                deps=[trs])
            tnb = b.eng("ve", lambda e: e.tensor_single_scalar(
                out=nb_t[:, :], in_=nb_t[:, :], scalar=-1.0, op=ALU.mult))
            tact = b.eng("sc", (lambda e, xt=xt, xb=xb: e.activation(
                out=xb, in_=xt, func=AF.Identity, bias=nb_t[:, :],
                scale=rstd_t[:, :])), deps=[tnb, prev_wr[p]])
            prev_ld[p] = tact
            wr = b.dma("sc", x1tm[tc * 128:(tc + 1) * 128, :], xb, deps=[tact], sem=f"x1tm{p}")
            prev_wr[p] = wr
            toks.append(wr)
        return toks

    def emit_dequant(codes_ap, am_ap, cache_ap, nrow_chunks, ktiles, state):
        for rc in range(nrow_chunks):
            for (ks, kn) in ktiles:
                p = state["ti"] % 2
                nb = kn // 64
                ld_c = b.dma("sp", qcodes[p][:, :kn],
                             codes_ap[rc * 128:(rc + 1) * 128, ks:ks + kn],
                             deps=[state["code_rd"][p]], sem=f"qc{p}")
                ld_a = b.dma("sc", qam[p][:, :nb],
                             am_ap[rc * 128:(rc + 1) * 128, ks // 64:ks // 64 + nb],
                             deps=[state["w_wr"][p]], sem=f"qa{p}")
                tv = b.eng("ve", (lambda e, p=p, kn=kn: e.tensor_scalar(
                    out=qv[:, :kn], in0=qcodes[p][:, :kn], scalar1=VSCALE,
                    scalar2=VSHIFT, op0=ALU.mult, op1=ALU.add)), deps=[ld_c])
                state["code_rd"][p] = tv
                b.eng("ve", (lambda e, kn=kn: e.tensor_single_scalar(
                    out=qu[0][:, :kn], in_=qv[:, :kn], scalar=BS[0], op=ALU.mult)))
                ping = 0
                for bk in BS[1:]:
                    ping ^= 1
                    b.eng("ve", (lambda e, bk=bk, ping=ping, kn=kn:
                                 e.scalar_tensor_tensor(
                                     out=qu[ping][:, :kn], in0=qu[ping ^ 1][:, :kn],
                                     scalar=bk, in1=qv[:, :kn],
                                     op0=ALU.add, op1=ALU.mult)))
                tw = b.eng("ve", (lambda e, p=p, ping=ping, kn=kn, nb=nb:
                                  e.scalar_tensor_tensor(
                                      out=qw[p][:, :kn].rearrange("p (a c) -> p a c", a=nb),
                                      in0=qu[ping][:, :kn].rearrange("p (a c) -> p a c", a=nb),
                                      scalar=A0,
                                      in1=qam[p][:, :nb, None].broadcast_to([128, nb, 64]),
                                      op0=ALU.add, op1=ALU.mult)),
                           deps=[ld_a, state["w_wr"][p]])
                wr = b.dma("gp", cache_ap[rc * 128:(rc + 1) * 128, ks:ks + kn],
                           qw[p][:, :kn], deps=[tw], sem=f"qw{p}")
                state["w_wr"][p] = wr
                state["ti"] += 1

    dq_state = {"ti": 0, "code_rd": [None, None], "w_wr": [None, None]}
    groups = [list(range(C.NCORES))]
    ln1 = emit_ln(x_in, init_dep=lad_tok)
    DQS = 12 * C.QKMAX + 2 * (C.QKMAX // 64) + 256
    KC_EARLY = [kc for kc in range(C.KC) if kc * T >= DQS]
    emit_dequant(gc, ga, shard_g, C.GU_SH_CH, C.QK_GU, dq_state)
    ag_g = b.collective("AllGather", ALU.bypass, groups, [shard_g], [gcache],
                        deps=[dq_state["w_wr"][0], dq_state["w_wr"][1]], sem="agg")
    for kc in KC_EARLY:
        b.dma("sp", x1fm(kc), x1tm[:, kc * 128:(kc + 1) * 128],
              deps=ln1[-2:], transpose=True, sem="xfm")
    emit_dequant(uc, ua, shard_u, C.GU_SH_CH, C.QK_GU, dq_state)
    ag_u = b.collective("AllGather", ALU.bypass, groups, [shard_u], [ucache],
                        deps=[dq_state["w_wr"][0], dq_state["w_wr"][1]], sem="agu")
    emit_dequant(dc, da, shard_d, C.D_SH_CH, C.QK_D, dq_state)
    ag_d = b.collective("AllGather", ALU.bypass, groups, [shard_d], [dcache],
                        deps=[dq_state["w_wr"][0], dq_state["w_wr"][1]], sem="agd")
    fm_last = None
    for kc in range(C.KC):
        if kc in KC_EARLY:
            continue
        fm_last = b.dma("sp", x1fm(kc), x1tm[:, kc * 128:(kc + 1) * 128],
                        deps=ln1[-2:] + [dq_state["w_wr"][0], dq_state["w_wr"][1]],
                        transpose=True, sem="xfm")

    def emit_ffn(app, resid_src, resid_dst, x1_toks, fm_pre=None,
                 strip_dep=(None, None), dwt_dep=None):
        if fm_pre is not None:
            fm_all = fm_pre
        else:
            fm_all = None
            for kc in range(C.KC):
                fm_all = b.dma("sp", x1fm(kc), x1tm[:, kc * 128:(kc + 1) * 128],
                               deps=x1_toks[-2:], transpose=True, sem="xfm")
        t1_prev = None
        for (la, la_tok, t1sb) in ((la_g, lag_tok, t1_g), (la_u, lau_tok, t1_u)):
            for (ts, tn) in C.TPIECES:
                mm = None
                for kc in range(C.KC):
                    mm = b.eng("pe", (lambda e, kc=kc, ts=ts, tn=tn, la=la:
                                      e.matmul(pb[4][:R, :tn],
                                               la[:, kc * R:(kc + 1) * R],
                                               x1fm(kc)[:, ts:ts + tn],
                                               start=(kc == 0), stop=(kc == C.KC - 1))),
                               deps=[fm_all, la_tok] + ([t1_prev] if kc == 0 else []))
                t1_prev = b.eng("ve", (lambda e, t1sb=t1sb, ts=ts, tn=tn: e.tensor_copy(
                    out=t1sb[:, ts:ts + tn], in_=pb[4][:R, :tn])), deps=[mm])
        t1done = t1_prev

        def load_strip(which, f0, fn, dep):
            cache = gcache if which == "g" else ucache
            stf = gstrip if which == "g" else ustrip
            tok = None
            for kc in range(C.KC):
                tok = b.dma("sp", stf(kc)[:, :fn],
                            cache[f0:f0 + fn, kc * 128:(kc + 1) * 128],
                            deps=(dep if isinstance(dep, list) else [dep]),
                            transpose=True, sem=f"st{which}")
            return (tok,)

        lb_busy = [None]

        def load_lb(src, dst, c0, cn, war_dep):
            l1 = b.dma("sc", lb_f32[:, :cn], src[:, c0:c0 + cn], deps=[lb_busy[0]], sem="lb")
            l2 = b.eng("ve", (lambda e, dst=dst, cn=cn: e.tensor_copy(
                out=dst[:, :cn], in_=lb_f32[:, :cn])), deps=[l1, war_dep])
            lb_busy[0] = l2
            return l2

        war = {}
        x3_last = {}
        gstrip_all = load_strip("g", *C.FGROUPS[0], dep=strip_dep[0])
        ustrip_all = load_strip("u", *C.FGROUPS[0], dep=strip_dep[1])
        for gi, (f0, fn) in enumerate(C.FGROUPS):
            lbg = load_lb(glb, lb_g, f0, fn, war.get("lbg"))
            lbu = load_lb(ulb, lb_u, f0, fn, war.get("lbu"))
            relu_toks = []
            y1_last = None
            for tc in range(C.TC):
                bank = pb[tc % 2]
                for kc in range(C.KC):
                    b.eng("pe", (lambda e, kc=kc, tc=tc, fn=fn, bank=bank:
                                 e.matmul(bank[:, :fn],
                                          x1fm(kc)[:, tc * 128:(tc + 1) * 128],
                                          gstrip(kc)[:, :fn],
                                          start=(kc == 0), stop=False)),
                          deps=(list(gstrip_all) + [fm_all, t1done,
                                 war.get(("y1", tc % 2))] if kc == 0 else []))
                mm = b.eng("pe", (lambda e, tc=tc, fn=fn, bank=bank:
                                  e.matmul(bank[:, :fn],
                                           t1_g[:, tc * 128:(tc + 1) * 128],
                                           lb_g[:R, :fn], start=False, stop=True)),
                           deps=[lbg, t1done])
                war["lbg"] = mm
                y1_last = mm
                rt = b.eng("sc", (lambda e, tc=tc, fn=fn, bank=bank: e.activation(
                    out=ry1[tc][:, :fn], in_=bank[:, :fn], func=AF.Relu)),
                    deps=[mm, war.get(("ry", tc))])
                relu_toks.append(rt)
                war[("y1", tc % 2)] = rt
            if gi + 1 < len(C.FGROUPS):
                nf0, nfn = C.FGROUPS[gi + 1]
                gstrip_next = load_strip("g", nf0, nfn, y1_last)
            y2_last = None
            for tc in range(C.TC):
                bank = pb[2 + tc % 2]
                for kc in range(C.KC):
                    b.eng("pe", (lambda e, kc=kc, tc=tc, fn=fn, bank=bank:
                                 e.matmul(bank[:, :fn],
                                          x1fm(kc)[:, tc * 128:(tc + 1) * 128],
                                          ustrip(kc)[:, :fn],
                                          start=(kc == 0), stop=False)),
                          deps=(list(ustrip_all) + [t1done,
                                 war.get(("y2", tc % 2))] if kc == 0 else []))
                mm = b.eng("pe", (lambda e, tc=tc, fn=fn, bank=bank:
                                  e.matmul(bank[:, :fn],
                                           t1_u[:, tc * 128:(tc + 1) * 128],
                                           lb_u[:R, :fn], start=False, stop=True)),
                           deps=[lbu, t1done])
                war["lbu"] = mm
                y2_last = mm
                xp = tc % 2
                x3t = b.eng("ve", (lambda e, tc=tc, fn=fn, bank=bank, xp=xp:
                                   e.tensor_tensor(out=x3st[xp][:, :fn],
                                                   in0=ry1[tc][:, :fn],
                                                   in1=bank[:, :fn], op=ALU.mult)),
                            deps=[relu_toks[tc], mm, war.get(("x3", xp))])
                war[("ry", tc)] = x3t
                war[("y2", tc % 2)] = x3t
                wr = b.dma("gp", x3tm[tc * 128:(tc + 1) * 128, f0:f0 + fn],
                           x3st[xp][:, :fn], deps=[x3t], sem=f"x3tm{xp}")
                war[("x3", xp)] = wr
                x3_last[xp] = wr
            if gi + 1 < len(C.FGROUPS):
                ustrip_all = load_strip("u", nf0, nfn, y2_last)
                gstrip_all = gstrip_next

        out_toks = []
        rw_last = {}
        dwar = {}
        for h in range(2):
            hrow = h * C.TH
            x3f_all = None
            for jc in range(C.FC):
                x3f_all = b.dma("sp", x3fm(jc),
                                x3tm[hrow:hrow + C.TH, jc * 128:(jc + 1) * 128],
                                deps=list(x3_last.values()) + list(rw_last.values()),
                                transpose=True, sem="x3fm")
            mm = None
            for jc in range(C.FC):
                mm = b.eng("pe", (lambda e, jc=jc: e.matmul(
                    pb[4][:R, :C.TH], la_d[:, jc * R:(jc + 1) * R], x3fm(jc)[:, :],
                    start=(jc == 0), stop=(jc == C.FC - 1))),
                    deps=[x3f_all, lad_tok] + ([dwar.get("t1d")] if jc == 0 else []))
            t1d_ev = b.eng("ve", lambda e: e.tensor_copy(out=t1_d[:, :],
                                                         in_=pb[4][:R, :C.TH]),
                           deps=[mm])
            dwar["t1d"] = t1d_ev
            for oi, (o0, on) in enumerate(C.OGROUPS):
                lbd = load_lb(dlb, lb_d, o0, on, dwar.get("lbd"))
                banks = [pb[(oi % 2) * 4 + i] for i in range(C.THC)]
                last_mm = [None] * C.THC
                for jc in range(C.FC):
                    p = jc % 4
                    ldw = b.dma("sp", dwt[p][:, :on],
                                dcache[o0:o0 + on, jc * 128:(jc + 1) * 128],
                                deps=[dwar.get(("dw", p)), dwt_dep],
                                transpose=True, sem=f"dw{p}")
                    for tcl in range(C.THC):
                        last_mm[tcl] = b.eng(
                            "pe", (lambda e, jc=jc, tcl=tcl, on=on, p=p, bank=banks[tcl]:
                                   e.matmul(bank[:, :on],
                                            x3fm(jc)[:, tcl * 128:(tcl + 1) * 128],
                                            dwt[p][:, :on],
                                            start=(jc == 0), stop=False)),
                            deps=[ldw, x3f_all, t1d_ev]
                            + ([dwar.get(("bk", oi % 2, tcl))] if jc == 0 else []))
                    dwar[("dw", p)] = last_mm[C.THC - 1]
                for tcl in range(C.THC):
                    lm = b.eng("pe", (lambda e, tcl=tcl, on=on, bank=banks[tcl]:
                                      e.matmul(bank[:, :on],
                                               t1_d[:, tcl * 128:(tcl + 1) * 128],
                                               lb_d[:R, :on], start=False, stop=True)),
                               deps=[lbd, t1d_ev])
                    dwar["lbd"] = lm
                    xp = (oi * C.THC + tcl) % 4
                    ldx = b.dma("sc", xr_t[xp][:, :on],
                                resid_src[hrow + tcl * 128:hrow + (tcl + 1) * 128,
                                          o0:o0 + on],
                                deps=[dwar.get(("xr", xp))], sem=f"xr{xp}")
                    ev = b.eng("ve", (lambda e, on=on, bank=banks[tcl], xp=xp:
                                      e.scalar_tensor_tensor(
                                          out=nx_t[xp][:, :on], in0=bank[:, :on],
                                          scalar=0.5, in1=xr_t[xp][:, :on],
                                          op0=ALU.mult, op1=ALU.add)),
                               deps=[lm, ldx, dwar.get(("nx", xp))])
                    dwar[("bk", oi % 2, tcl)] = ev
                    dwar[("xr", xp)] = ev
                    wr = b.dma("gp", resid_dst[hrow + tcl * 128:hrow + (tcl + 1) * 128,
                                               o0:o0 + on],
                               nx_t[xp][:, :on], deps=[ev], sem=f"rw{xp}")
                    dwar[("nx", xp)] = wr
                    rw_last[xp] = wr
                    out_toks.append(wr)
        return list(rw_last.values())

    def emit_attn(z_toks, resid_src, resid_dst):
        zf_all = None
        for kc in range(C.KC):
            zf_all = b.dma("sp", x1fm(kc), x1tm[:, kc * 128:(kc + 1) * 128],
                           deps=z_toks[-2:], transpose=True, sem="xfm")
        tile_idx = {st: i for i, st in enumerate(band_tiles)}
        out_toks = []
        prev = {}
        for g0 in range(0, C.NT, C.TGT):
            gts, sigs = group_sigs[g0]
            gcols = len(gts) * 128
            smin = sigs[0]
            sspan = sigs[-1] - smin + 1
            cons_last = {}
            for kc in range(C.KC):
                p = kc % 2
                lw1 = b.dma("sp", w1t[p][:, :sspan * 128],
                            w1b[kc * 128:(kc + 1) * 128, smin * 128:(smin + sspan) * 128],
                            deps=[wcast_all, prev.get(("w1t", p))], sem=f"w1t{p}")
                lw2 = b.dma("sp", w2t[p][:, :gcols],
                            w2b[g0 * 128:g0 * 128 + gcols, kc * 128:(kc + 1) * 128],
                            deps=[wcast_all, prev.get(("w2t", p))], transpose=True, sem=f"w2t{p}")
                for si, sg in enumerate(sigs):
                    mm = b.eng("pe", (lambda e, si=si, sg=sg, p=p, gcols=gcols, kc=kc,
                                      smin=smin:
                                      e.matmul(pb[si][:, :gcols],
                                               w1t[p][:, (sg - smin) * 128:(sg - smin + 1) * 128],
                                               w2t[p][:, :gcols],
                                               start=(kc == 0), stop=(kc == C.KC - 1))),
                               deps=[lw1, lw2]
                               + ([prev.get(("cons", si)), prev.get(("y", 0)),
                                   prev.get(("y", 1))] if kc == 0 else []))
                    cons_last[si] = mm
                prev[("w1t", p)] = cons_last[len(sigs) - 1]
                prev[("w2t", p)] = cons_last[len(sigs) - 1]
            ev_toks = {}
            mi = 0
            for si, sg in enumerate(sigs):
                taus = sorted(t for t in gts if (sg, t) in tile_idx)
                last_ev = None
                for t in taus:
                    mp = mi % 2
                    lm = b.dma("sc", mk_t[mp][:, :], maskb[tile_idx[(sg, t)], :, :],
                               deps=[prev.get(("mk", mp))], sem=f"mk{mp}")
                    col = (si * C.TGT + (t - g0)) * 128
                    evt = b.eng("ve", (lambda e, si=si, t=t, g0=g0, col=col, mp=mp:
                                       e.tensor_tensor(
                                           out=a_sb[:, col:col + 128],
                                           in0=pb[si][:, (t - g0) * 128:(t - g0 + 1) * 128],
                                           in1=mk_t[mp][:, :], op=ALU.mult)),
                                deps=[cons_last[si], lm, prev.get(("asb", si, t))])
                    prev[("mk", mp)] = evt
                    prev[("asb", si, t)] = evt
                    ev_toks[(sg, t)] = evt
                    last_ev = evt
                    mi += 1
                prev[("cons", si)] = last_ev
            lb2 = b.dma("sc", b2row[:1, :gcols],
                        b2[g0 * 128:g0 * 128 + gcols].rearrange("(a n) -> a n", a=1),
                        deps=[prev.get("b2"), prev.get("b2mm")], sem="b2l")
            cb2 = b.eng("ve", (lambda e, gcols=gcols: e.tensor_copy(
                out=b2row_bf[:1, :gcols], in_=b2row[:1, :gcols])), deps=[lb2])
            prev["b2"] = cb2
            all_ev = list(ev_toks.values())
            for tc in range(C.TC):
                ybi = 6 + (tc % 2) if nsig_max <= 6 else (tc % 2)
                yb = pb[ybi]
                plan = []
                for sg in sigs:
                    taus = sorted(t for t in gts if (sg, t) in tile_idx)
                    if taus:
                        plan.append((sg, taus))
                dep0 = [cb2, prev.get(("y", tc % 2))] + all_ev
                bmm = b.eng("pe", (lambda e, gcols=gcols, yb=yb: e.matmul(
                    yb[:, :gcols], ones_bf[:1, :], b2row_bf[:1, :gcols],
                    start=True, stop=False)), deps=dep0)
                prev["b2mm"] = bmm
                for pi, (sg, taus) in enumerate(plan):
                    c0 = (taus[0] - g0) * 128
                    c1 = (taus[-1] - g0 + 1) * 128
                    si = sigs.index(sg)
                    acol = (si * C.TGT + (taus[0] - g0)) * 128
                    last = (pi == len(plan) - 1)
                    mm = b.eng("pe", (lambda e, sg=sg, c0=c0, c1=c1, tc=tc,
                                      acol=acol, yb=yb, last=last:
                                      e.matmul(yb[:, c0:c1],
                                               x1fm(sg)[:, tc * 128:(tc + 1) * 128],
                                               a_sb[:, acol:acol + (c1 - c0)],
                                               start=False, stop=last)),
                               deps=[zf_all])
                yp = tc % 2
                gl = b.eng("sc", (lambda e, gcols=gcols, yb=yb, yp=yp: e.activation(
                    out=yg_t[yp][:, :gcols], in_=yb[:, :gcols], func=AF.Gelu)),
                    deps=[mm, prev.get(("yg", yp))])
                xp = tc % 4
                ldx = b.dma("sc", xr_t[xp][:, :gcols],
                            resid_src[tc * 128:(tc + 1) * 128, g0 * 128:g0 * 128 + gcols],
                            deps=[prev.get(("axr", xp))], sem=f"xr{xp}")
                ad = b.eng("ve", (lambda e, gcols=gcols, xp=xp, yp=yp: e.tensor_tensor(
                    out=nx_t[xp][:, :gcols], in0=yg_t[yp][:, :gcols],
                    in1=xr_t[xp][:, :gcols], op=ALU.add)),
                    deps=[gl, ldx, prev.get(("anx", xp))])
                prev[("y", tc % 2)] = ad
                prev[("yg", yp)] = ad
                wr = b.dma("gp", resid_dst[tc * 128:(tc + 1) * 128,
                                           g0 * 128:g0 * 128 + gcols],
                           nx_t[xp][:, :gcols], deps=[ad], sem=f"rw{xp}")
                prev[("axr", xp)] = ad
                prev[("anx", xp)] = wr
                out_toks.append(wr)
        return out_toks

    ffn1 = emit_ffn(1, x_in, resid2, ln1, fm_pre=fm_last,
                    strip_dep=([ag_g] + ln1[-2:], [ag_u] + ln1[-2:]), dwt_dep=ag_d)
    b.barrier()
    ln2 = emit_ln(resid2)
    b.barrier()
    attn = emit_attn(ln2, resid2, resid3)
    b.barrier()
    ln3 = emit_ln(resid3)
    b.barrier()
    ffn2 = emit_ffn(2, resid3, out, ln3)
    b.barrier()
    b.eng("gp", lambda e: e.engine_nop(), deps=ffn2)

    b.finalize()
    return nc


def kernel(L0_attn_x, L0_attn_w1, L0_attn_w2, L0_attn_b2, L0_attn_sparse_mask,
           L0_ffn_g_codes, L0_ffn_g_absmax, L0_ffn_g_lora_a, L0_ffn_g_lora_b,
           L0_ffn_u_codes, L0_ffn_u_absmax, L0_ffn_u_lora_a, L0_ffn_u_lora_b,
           L0_ffn_d_codes, L0_ffn_d_absmax, L0_ffn_d_lora_a, L0_ffn_d_lora_b):
    DM = 4096
    DFF = 11008
    Tfull = int(np.prod(L0_attn_x.shape[:-1]))
    T = Tfull // NCORES
    cfg = Cfg(DM, DFF, T)
    band = band_tiles_from_mask_shape(DM)

    x = np.ascontiguousarray(np.asarray(L0_attn_x, dtype=np.float32).reshape(Tfull, DM))
    mask = np.asarray(L0_attn_sparse_mask, dtype=np.float32)
    maskb = np.ascontiguousarray(
        np.stack([mask[s * 128:(s + 1) * 128, t * 128:(t + 1) * 128] for (s, t) in band]))

    def shard_rows(a, rows_per_core):
        a = np.asarray(a)
        total = rows_per_core * NCORES
        if a.shape[0] < total:
            pad = np.zeros((total - a.shape[0],) + a.shape[1:], dtype=a.dtype)
            a = np.concatenate([a, pad], axis=0)
        return [np.ascontiguousarray(a[c * rows_per_core:(c + 1) * rows_per_core])
                for c in range(NCORES)]

    GUR = cfg.GU_SH_CH * 128
    DR = cfg.D_SH_CH * 128
    gc_sh = shard_rows(np.asarray(L0_ffn_g_codes, dtype=np.int32), GUR)
    ga_sh = shard_rows(np.asarray(L0_ffn_g_absmax, dtype=np.float32), GUR)
    uc_sh = shard_rows(np.asarray(L0_ffn_u_codes, dtype=np.int32), GUR)
    ua_sh = shard_rows(np.asarray(L0_ffn_u_absmax, dtype=np.float32), GUR)
    dc_sh = shard_rows(np.asarray(L0_ffn_d_codes, dtype=np.int32), DR)
    da_sh = shard_rows(np.asarray(L0_ffn_d_absmax, dtype=np.float32), DR)

    common = {
        "w1": np.ascontiguousarray(np.asarray(L0_attn_w1, dtype=np.float32)),
        "w2": np.ascontiguousarray(np.asarray(L0_attn_w2, dtype=np.float32)),
        "maskb": maskb,
        "b2": np.ascontiguousarray(np.asarray(L0_attn_b2, dtype=np.float32)),

        "gla": np.ascontiguousarray(np.asarray(L0_ffn_g_lora_a, dtype=np.float32)),
        "glb": np.ascontiguousarray(np.asarray(L0_ffn_g_lora_b, dtype=np.float32)),
        "ula": np.ascontiguousarray(np.asarray(L0_ffn_u_lora_a, dtype=np.float32)),
        "ulb": np.ascontiguousarray(np.asarray(L0_ffn_u_lora_b, dtype=np.float32)),
        "dla": np.ascontiguousarray(np.asarray(L0_ffn_d_lora_a, dtype=np.float32)),
        "dlb": np.ascontiguousarray(np.asarray(L0_ffn_d_lora_b, dtype=np.float32)),
    }
    in_maps = []
    for c in range(NCORES):
        m = dict(common)
        m["x"] = np.ascontiguousarray(x[c * T:(c + 1) * T])
        m["gc"], m["ga"] = gc_sh[c], ga_sh[c]
        m["uc"], m["ua"] = uc_sh[c], ua_sh[c]
        m["dc"], m["da"] = dc_sh[c], da_sh[c]
        in_maps.append(m)

    nc = build_program(cfg, band)
    res = run_bass_kernel_spmd(nc, in_maps, core_ids=list(range(NCORES)))
    shards = [np.asarray(r["out"]).reshape(T, DM) for r in res.results]
    return np.concatenate(shards, axis=0).reshape(L0_attn_x.shape).astype(np.float32)
